# version 04
# baseline (speedup 1.0000x reference)
from contextlib import ExitStack

import numpy as np
import ml_dtypes

import concourse.bass as bass
import concourse.mybir as mybir
from concourse.bass_utils import run_bass_kernel_spmd

F32 = mybir.dt.float32
BF16 = mybir.dt.bfloat16
U8 = mybir.dt.uint8
AF = mybir.ActivationFunctionType
ALU = mybir.AluOpType
AX = mybir.AxisListType
BF = ml_dtypes.bfloat16

B, L, D = 4, 4096, 2048
DH = 1024
NST = 16
TC = 512
NT = L // TC
TG = 256
NG = NT
IF = 8192

C_EPS, C_INVD, C_NEG1, C_ONE, C_NEGH, C_NEG128, C_HALF = range(7)
NCONST = 8


class Sem:
    def __init__(self, es, nc, name):
        self.h = es.enter_context(nc.semaphore(name))
        self.n = 0

    def inc(self, instr, k=1):
        instr.then_inc(self.h, k)
        self.n += k
        return self.n


class _MockInstr:
    def then_inc(self, h, k):
        return self


class _MockEng:
    def __getattr__(self, name):
        return lambda *a, **kw: _MockInstr()


_UNCHAINED = {"wait_ge", "sem_inc", "dma_start", "collective_compute",
              "then_inc", "nop", "engine_nop"}


class ChainEng:

    def __init__(self, eng, sem):
        self._e = eng
        self._s = sem

    def __getattr__(self, name):
        f = getattr(self._e, name)
        if name in _UNCHAINED or name.startswith("_"):
            return f

        def g(*a, **kw):
            if self._s.n > 0:
                self._e.wait_ge(self._s.h, self._s.n)
            r = f(*a, **kw)
            self._s.inc(r, 1)
            return r
        return g


def build_nc():
    nc = bass.Bass("TRN2", target_bir_lowering=False)

    xb = nc.dram_tensor("xb", [L, D], F32, kind="ExternalInput").ap()
    x0m = nc.dram_tensor("x0m", [NG, TG, D], F32, kind="ExternalInput").ap()
    wdt = nc.dram_tensor("wdt", [128, 16, DH], F32, kind="ExternalInput").ap()
    wxp = nc.dram_tensor("wxp", [128, 16, 2 * NST], F32, kind="ExternalInput").ap()
    wout = nc.dram_tensor("wout", [4, 128, 8, 512], F32, kind="ExternalInput").ap()
    alog = nc.dram_tensor("alog", [128, 128], F32, kind="ExternalInput").ap()
    dtb = nc.dram_tensor("dtb", [128, 8], F32, kind="ExternalInput").ap()
    xpb = nc.dram_tensor("xpb", [2 * NST, 1], F32, kind="ExternalInput").ap()
    outb_bc = nc.dram_tensor("outb_bc", [128, D], F32, kind="ExternalInput").ap()
    wg8 = nc.dram_tensor("wg8", [64, 128, 16, 128], U8, kind="ExternalInput").ap()
    wu8 = nc.dram_tensor("wu8", [64, 128, 16, 128], U8, kind="ExternalInput").ap()
    wd8 = nc.dram_tensor("wd8", [4, 64, 128, 512], U8, kind="ExternalInput").ap()
    gsc = nc.dram_tensor("gsc", [128, 64], F32, kind="ExternalInput").ap()
    usc = nc.dram_tensor("usc", [128, 64], F32, kind="ExternalInput").ap()
    dsc_bc = nc.dram_tensor("dsc_bc", [128, D], F32, kind="ExternalInput").ap()
    ident = nc.dram_tensor("ident", [128, 128], BF16, kind="ExternalInput").ap()
    onesc = nc.dram_tensor("onesc", [128, 1], BF16, kind="ExternalInput").ap()
    onesr = nc.dram_tensor("onesr", [1, 512], BF16, kind="ExternalInput").ap()
    consts = nc.dram_tensor("consts", [128, NCONST], F32, kind="ExternalInput").ap()

    out = nc.dram_tensor("out", [NG, TG, D], F32, kind="ExternalOutput").ap()

    wg_bf = nc.dram_tensor("wg_bf", [64, 128, 16, 128], BF16)
    wu_bf = nc.dram_tensor("wu_bf", [64, 128, 16, 128], BF16)
    wd_bf = nc.dram_tensor("wd_bf", [4, 64, 128, 512], BF16)
    bc_d = nc.dram_tensor("bc_d", [2, 2 * NST, TC], BF16)
    mixp_d = nc.dram_tensor("mixp_d", [L, D], BF16)
    mixg_d = nc.dram_tensor("mixg_d", [NG, TG, D], BF16)

    es = ExitStack()
    sb = lambda name, shape, dt: es.enter_context(nc.sbuf_tensor(name, shape, dt))

    wdt_sb = sb("wdt_sb", [128, 16, DH], BF16)
    wxp_sb = sb("wxp_sb", [128, 16, 2 * NST], BF16)
    a_sb = sb("a_sb", [128, 128], F32)
    atmp_sb = sb("atmp_sb", [128, 128], F32)
    dtb_sb = sb("dtb_sb", [128, 8], F32)
    xpb_sb = sb("xpb_sb", [2 * NST, 1], F32)
    gsc_sb = sb("gsc_sb", [128, 64], F32)
    usc_sb = sb("usc_sb", [128, 64], F32)
    ident_sb = sb("ident_sb", [128, 128], BF16)
    onesc_sb = sb("onesc_sb", [128, 1], BF16)
    onesr_sb = sb("onesr_sb", [1, 512], BF16)
    ct = sb("ct", [128, NCONST], F32)
    sx = sb("sx", [128, 2], F32)
    sx2a = sb("sx2a", [128, 2], F32)
    sx2b = sb("sx2b", [128, 2], F32)
    smean = sb("smean", [128, 2], F32)
    snvar = sb("snvar", [128, 2], F32)
    srstd = sb("srstd", [128, 2], F32)
    snegmr = sb("snegmr", [128, 2], F32)

    base = nc._sbuf_addr_for_side("left")

    def at(name, shape, dt, cursor):
        bytes_pp = int(np.prod(shape[1:])) * mybir.dt.size(dt)
        off = (cursor[0] + 63) & ~63
        t = nc.alloc_sbuf_tensor_at(name, shape, dt, offset=off)
        cursor[0] = off + bytes_pp
        return t

    curab = [base]
    x0c = [at(f"x0c{i}", [128, D], F32, curab) for i in range(2)]
    ln1_tm = [at(f"ln1tm{i}", [128, D], BF16, curab) for i in range(2)]
    ln1_fm = at("ln1fm", [128, 16, TC], BF16, curab)
    dt_sb = at("dt_sb", [128, 8, TC], BF16, curab)
    u_sb = at("u_sb", [128, 8, TC], BF16, curab)
    bc_sb = [at(f"bc{i}", [2 * NST, TC], BF16, curab) for i in range(2)]
    b_bc = at("b_bc", [128, NST, TC], BF16, curab)
    c_bc = at("c_bc", [128, NST, TC], BF16, curab)
    da_sb = [at(f"da{i}", [128, TC], F32, curab) for i in range(2)]
    dbx_sb = at("dbx", [128, TC], BF16, curab)
    h_all = at("h_all", [128, NST, TC], BF16, curab)
    hstate = at("hstate", [128, 128], F32, curab)
    y32 = at("y32", [128, TC], F32, curab)
    y_bf = [at(f"ybf{i}", [128, 8, TC], BF16, curab) for i in range(2)]
    woutb = [at(f"woutb{i}", [128, 8, 512], BF16, curab) for i in range(2)]
    scr_ab = at("scr_ab", [128, 1024], F32, curab)
    mixp_sb = at("mixp_sb", [128, 4, D], BF16, curab)
    outbh_sb = at("outbh_sb", [128, D], BF16, curab)
    ab_end = curab[0]

    curc = [base]
    mixg_t = [at(f"mixg{i}", [128, 2, D], BF16, curc) for i in range(2)]
    x0m_t = at("x0m_t", [128, 2, D], F32, curc)
    x1_sb = at("x1_sb", [128, 2, D], BF16, curc)
    ln2_tm = at("ln2_tm", [128, 2, D], BF16, curc)
    ln2_fm = at("ln2_fm", [128, 16, TG], BF16, curc)
    wgb = [at(f"wgb{i}", [128, 16, 128], BF16, curc) for i in range(2)]
    wub = [at(f"wub{i}", [128, 16, 128], BF16, curc) for i in range(2)]
    wdb = [at(f"wdb{i}", [128, 8, 512], BF16, curc) for i in range(2)]
    g_b = [at(f"g_b{i}", [128, TG], BF16, curc) for i in range(2)]
    u_b = [at(f"u_b{i}", [128, TG], BF16, curc) for i in range(2)]
    hh_sb = at("hh_sb", [128, 64, TG], BF16, curc)
    rsn_sb = at("rsn_sb", [1, TG], BF16, curc)
    shn_sb = at("shn_sb", [1, TG], BF16, curc)
    ffn_tm = at("ffn_tm", [128, 2, D], BF16, curc)
    out_st = at("out_st", [128, 2, D], F32, curc)
    scr_c = at("scr_c", [128, 1024], F32, curc)
    dsc_sb = at("dsc_sb", [128, D], F32, curc)
    c_end = curc[0]
    assert max(ab_end, c_end) <= 238 * 1024, (base, ab_end, c_end)

    ps = [es.enter_context(nc.psum_tensor(f"ps{i}", [128, 512], F32)) for i in range(6)]
    pt0 = es.enter_context(nc.psum_tensor("pt0", [128, 1024], BF16))
    psr = es.enter_context(nc.psum_tensor("psr", [128, 512], F32))

    sConst = Sem(es, nc, "sConst")
    sX0 = [Sem(es, nc, f"sX0{i}") for i in range(2)]
    sMg = [Sem(es, nc, f"sMg{i}") for i in range(2)]
    sX0m = Sem(es, nc, "sX0m")
    sGU = [Sem(es, nc, f"sGU{i}") for i in range(2)]
    sWd = [Sem(es, nc, f"sWd{i}") for i in range(2)]
    sWout = [Sem(es, nc, f"sWout{i}") for i in range(2)]
    sDsc = Sem(es, nc, "sDsc")
    sWcast = Sem(es, nc, "sWcast")
    sPc = Sem(es, nc, "sPc")
    sMixp = Sem(es, nc, "sMixp")
    sBcd = Sem(es, nc, "sBcd")
    sBcb = Sem(es, nc, "sBcb")
    sST = Sem(es, nc, "sST")
    sCC = Sem(es, nc, "sCC")
    sPE = Sem(es, nc, "sPE")
    sVE = Sem(es, nc, "sVE")
    sSC = Sem(es, nc, "sSC")
    sDR = Sem(es, nc, "sDR")
    ALL_SEMS = ([sConst, sX0m, sDsc, sWcast, sPc, sMixp, sBcd, sBcb, sST, sCC,
                 sPE, sVE, sSC, sDR] + sX0 + sMg + sGU + sWd + sWout)

    ev = {}

    def snap(key, sem):
        order.append(("__snap__", (key, sem)))

    def wait(eng, key):
        sem, n = ev[key]
        if n > 0:
            eng.wait_ge(sem.h, n)

    sched = {"sync": [], "gpsimd": [], "scalar": [], "vector": [], "tensor": []}
    order = []

    def em(engine, fn):
        sched[engine].append(fn)
        order.append((engine, fn))

    with es:
        blk = es.enter_context(nc.Block())

        def _loads(sync):
            for ap_in, tile in ((alog, atmp_sb), (dtb, dtb_sb), (xpb, xpb_sb),
                                (gsc, gsc_sb), (usc, usc_sb), (ident, ident_sb),
                                (onesc, onesc_sb), (onesr, onesr_sb), (consts, ct)):
                sConst.inc(sync.dma_start(out=tile[:], in_=ap_in[:]), 16)
        em("sync", _loads)
        snap("loads0", sConst)

        def _casts(gpsimd):
            sWcast.inc(gpsimd.dma_start(out=wdt_sb[:].rearrange('p k c -> p (k c)'), in_=wdt[:].rearrange('p k c -> p (k c)')), 16)
            sWcast.inc(gpsimd.dma_start(out=outbh_sb[:], in_=outb_bc[:]), 16)
            sWcast.inc(gpsimd.dma_start(out=wxp_sb[:].rearrange('p k c -> p (k c)'), in_=wxp[:].rearrange('p k c -> p (k c)')), 16)
        em("gpsimd", _casts)
        snap("wcast", sWcast)

        def _precast(gpsimd):
            def flat(ap, cols=32768):
                n = 1
                for s in ap.shape:
                    n *= s
                return ap.rearrange(
                    " ".join(f"d{i}" for i in range(len(ap.shape)))
                    + " -> (" + " ".join(f"d{i}" for i in range(len(ap.shape)))
                    + ")").rearrange("(x y) -> x y", y=cols)
            for srcw, dst in ((wg8, wg_bf), (wu8, wu_bf)):
                for q in range(4):
                    sPc.inc(gpsimd.dma_start(
                        out=flat(dst.ap()[q * 16:(q + 1) * 16]),
                        in_=flat(srcw[q * 16:(q + 1) * 16])), 16)
            for ho in range(4):
                sPc.inc(gpsimd.dma_start(out=flat(wd_bf.ap()[ho]),
                                         in_=flat(wd8[ho])), 16)
        em("gpsimd", _precast)
        snap("precast", sPc)

        def _prep_sc(scalar):
            wait(scalar, "loads0")
            (scalar.activation(out=a_sb[:], in_=atmp_sb[:], func=AF.Exp))
        em("scalar", _prep_sc)
        snap("aexp", sSC)

        def _prep_ve(vector):
            wait(vector, "aexp")
            wait(vector, "wcast")
            vector.tensor_scalar(out=a_sb[:], in0=a_sb[:],
                                 scalar1=ct[:, C_NEG1:C_NEG1 + 1],
                                 scalar2=None, op0=ALU.mult)
            (vector.tensor_scalar(out=outbh_sb[:], in0=outbh_sb[:],
                                         scalar1=ct[:, C_HALF:C_HALF + 1],
                                         scalar2=None, op0=ALU.mult))
        em("vector", _prep_ve)
        snap("prep", sVE)

        ln_ctr = [0]
        ln_norm_key = {}

        def emit_ln_stats(src_ap_fn, scr):
            s = ln_ctr[0] % 2
            i = ln_ctr[0]
            ln_ctr[0] += 1

            def _v1(vector, i=i, s=s):
                if i >= 2:
                    wait(vector, f"lnstat_done{i - 2}")
                (vector.tensor_reduce(out=sx[:, s:s + 1], in_=src_ap_fn(),
                                             axis=AX.X, op=ALU.add))
            em("vector", _v1)
            snap(f"ln_sx{i}", sVE)

            def _s1(scalar, i=i, s=s):
                if i >= 2:
                    wait(scalar, f"lnstat_done{i - 2}")
                src = src_ap_fn()
                scalar.activation(out=scr[:], in_=src[:, 0:1024], func=AF.Square,
                                  accum_out=sx2a[:, s:s + 1])
                (scalar.activation(out=scr[:], in_=src[:, 1024:2048],
                                          func=AF.Square, accum_out=sx2b[:, s:s + 1]))
            em("scalar", _s1)
            snap(f"ln_q_{i}", sSC)

            def _v2(vector, i=i, s=s):
                wait(vector, f"ln_q_{i}")
                vector.tensor_scalar(out=smean[:, s:s + 1], in0=sx[:, s:s + 1],
                                     scalar1=ct[:, C_INVD:C_INVD + 1], scalar2=None,
                                     op0=ALU.mult)
                vector.tensor_tensor(out=snvar[:, s:s + 1], in0=sx2a[:, s:s + 1],
                                     in1=sx2b[:, s:s + 1], op=ALU.add)
                vector.tensor_scalar(out=snvar[:, s:s + 1], in0=snvar[:, s:s + 1],
                                     scalar1=ct[:, C_INVD:C_INVD + 1], scalar2=None,
                                     op0=ALU.mult)
                (vector.scalar_tensor_tensor(out=snvar[:, s:s + 1],
                                                    in0=smean[:, s:s + 1],
                                                    scalar=smean[:, s:s + 1],
                                                    in1=snvar[:, s:s + 1],
                                                    op0=ALU.mult, op1=ALU.subtract))
            em("vector", _v2)
            snap(f"ln_nvar{i}", sVE)

            def _s2(scalar, i=i, s=s):
                wait(scalar, f"ln_nvar{i}")
                scalar.activation(out=srstd[:, s:s + 1], in_=snvar[:, s:s + 1],
                                  func=AF.Ln, scale=ct[:, C_NEG1:C_NEG1 + 1],
                                  bias=ct[:, C_EPS:C_EPS + 1])
                (scalar.activation(out=srstd[:, s:s + 1], in_=srstd[:, s:s + 1],
                                          func=AF.Exp, scale=ct[:, C_NEGH:C_NEGH + 1]))
            em("scalar", _s2)
            snap(f"ln_rstd{i}", sSC)

            def _v3(vector, i=i, s=s):
                wait(vector, f"ln_rstd{i}")
                if i - 2 in ln_norm_key:
                    wait(vector, ln_norm_key[i - 2])
                (vector.scalar_tensor_tensor(out=snegmr[:, s:s + 1],
                                                    in0=smean[:, s:s + 1],
                                                    scalar=ct[:, C_NEG1:C_NEG1 + 1],
                                                    in1=srstd[:, s:s + 1],
                                                    op0=ALU.mult, op1=ALU.mult))
            em("vector", _v3)
            snap(f"lnstat_done{i}", sVE)
            return s, i

        for c in range(NT):
            for sidx in range(4):
                si = c * 4 + sidx
                xi = si % 2
                r0 = c * TC + sidx * 128

                def _ld(sync, xi=xi, r0=r0, si=si):
                    if si >= 2:
                        wait(sync, f"x0done{si - 2}")
                    sX0[xi].inc(sync.dma_start(out=x0c[xi][:],
                                               in_=xb[r0:r0 + 128, :]), 16)
                em("sync", _ld)
                snap(f"x0ld{si}", sX0[xi])

                def _wv(vector, k=f"x0ld{si}"):
                    wait(vector, k)
                em("vector", _wv)

                def _ws(scalar, k=f"x0ld{si}"):
                    wait(scalar, k)
                em("scalar", _ws)
                slot, li = emit_ln_stats(lambda xi=xi: x0c[xi][:], scr_ab)

                def _norm(scalar, xi=xi, slot=slot, li=li, si=si):
                    wait(scalar, f"lnstat_done{li}")
                    if si >= 2:
                        wait(scalar, f"tr{(si - 2) * 2 + 1}")
                    (scalar.activation(out=ln1_tm[xi][:], in_=x0c[xi][:],
                                              func=AF.Identity,
                                              scale=srstd[:, slot:slot + 1],
                                              bias=snegmr[:, slot:slot + 1]))
                em("scalar", _norm)
                snap(f"ln1tm{si}", sSC)
                snap(f"x0done{si}", sSC)
                ln_norm_key[li] = f"ln1tm{si}"

                for grp in range(2):
                    gi = si * 2 + grp

                    def _tr(tensor, xi=xi, grp=grp, gi=gi, si=si, c=c):
                        if gi == 0:
                            wait(tensor, "loads0")
                        wait(tensor, f"ln1tm{si}")
                        if gi >= 1:
                            wait(tensor, f"trevac{gi - 1}")
                        last = None
                        for dcq in range(8):
                            dc = grp * 8 + dcq
                            last = tensor.transpose(pt0[:, dcq * 128:(dcq + 1) * 128],
                                                    ln1_tm[xi][:, dc * 128:(dc + 1) * 128],
                                                    ident_sb[:])
                        sPE.inc(last)
                    em("tensor", _tr)
                    snap(f"tr{gi}", sPE)

                    def _trev(scalar, grp=grp, gi=gi, sidx=sidx, c=c):
                        wait(scalar, f"tr{gi}")
                        if c >= 1 and sidx == 0 and grp == 0:
                            wait(scalar, f"udone{c - 1}")
                        dc0 = grp * 8
                        (scalar.copy(
                            ln1_fm[:, dc0:dc0 + 8, sidx * 128:(sidx + 1) * 128],
                            pt0[:].rearrange("p (dc t) -> p dc t", dc=8)))
                    em("scalar", _trev)
                    snap(f"trevac{gi}", sSC)
            snap(f"ln1fm{c}", sSC)

            for m in range(8):
                gdm = c * 8 + m

                def _mm(tensor, c=c, m=m, gdm=gdm):
                    if c == 0 and m == 0:
                        wait(tensor, "wcast")
                    wait(tensor, f"ln1fm{c}")
                    if gdm >= 2:
                        wait(tensor, f"dtevac{gdm - 2}")
                    pm = ps[m % 2]
                    last = None
                    for kk in range(16):
                        last = tensor.matmul(pm[:], wdt_sb[:, kk, m * 128:(m + 1) * 128],
                                             ln1_fm[:, kk, :],
                                             start=(kk == 0), stop=(kk == 15))
                    sPE.inc(last)
                em("tensor", _mm)
                snap(f"dtmm{gdm}", sPE)

                def _sp(scalar, c=c, m=m, gdm=gdm):
                    wait(scalar, f"dtmm{gdm}")
                    if c > 0 and m == 0:
                        wait(scalar, f"udone{c - 1}")
                    e = scr_ab[:, (m % 2) * 512:(m % 2) * 512 + 512]
                    scalar.activation(out=e, in_=ps[m % 2][:], func=AF.Exp,
                                      bias=dtb_sb[:, m:m + 1])
                    (scalar.activation(out=dt_sb[:, m, :], in_=e, func=AF.Ln,
                                              bias=ct[:, C_ONE:C_ONE + 1]))
                em("scalar", _sp)
                snap(f"dtevac{gdm}", sSC)
            snap(f"dt{c}", sSC)

            def _xp(tensor, c=c):
                wait(tensor, f"ln1fm{c}")
                if c >= 2:
                    wait(tensor, f"bcevac{c - 2}")
                last = None
                for kk in range(16):
                    last = tensor.matmul(ps[2][0:2 * NST, :], wxp_sb[:, kk, :],
                                         ln1_fm[:, kk, :],
                                         start=(kk == 0), stop=(kk == 15))
                sPE.inc(last)
            em("tensor", _xp)
            snap(f"xpmm{c}", sPE)

            def _xpe(scalar, c=c):
                wait(scalar, f"xpmm{c}")
                if c >= 2:
                    wait(scalar, f"bcst{c - 2}")
                (scalar.activation(out=bc_sb[c % 2][:], in_=ps[2][0:2 * NST, :],
                                          func=AF.Identity, bias=xpb_sb[:]))
            em("scalar", _xpe)
            snap(f"bcevac{c}", sSC)

            def _u(vector, c=c):
                wait(vector, f"dt{c}")
                last = None
                for m in range(8):
                    last = vector.tensor_tensor(out=u_sb[:, m, :], in0=dt_sb[:, m, :],
                                                in1=ln1_fm[:, m, :], op=ALU.mult)
                (last)
            em("vector", _u)
            snap(f"udone{c}", sVE)

            def emit_outproj(cc):
                for mo in range(4):
                    def _wl(gpsimd, cc=cc, mo=mo):
                        gi4 = cc * 4 + mo
                        if gi4 >= 2:
                            wait(gpsimd, f"opmm{cc}_{(mo - 2) * 4 + 3}" if mo >= 2
                                 else f"opmm{cc - 1}_{(mo + 2) * 4 + 3}")
                        sWout[mo % 2].inc(gpsimd.dma_start(
                            out=woutb[mo % 2][:].rearrange('p k c -> p (k c)'),
                            in_=wout[mo].rearrange('p k c -> p (k c)')), 16)
                    em("gpsimd", _wl)
                    snap(f"opwl{cc}_{mo}", sWout[mo % 2])

                    for tt in range(4):
                        gi = mo * 4 + tt

                        def _g(tensor, cc=cc, tt=tt, mo=mo, gi=gi):
                            if gi == 0:
                                wait(tensor, f"ycast{cc}")
                                wait(tensor, "prep")
                            if tt == 0:
                                wait(tensor, f"opwl{cc}_{mo}")
                            if gi >= 2:
                                wait(tensor, f"opev{cc}_{gi - 2}")
                            pm = ps[3] if gi % 2 == 0 else psr
                            last = None
                            for kk in range(8):
                                last = tensor.matmul(
                                    pm[:], y_bf[cc % 2][:, kk, tt * 128:(tt + 1) * 128],
                                    woutb[mo % 2][:, kk, :],
                                    start=(kk == 0), stop=(kk == 7))
                            sPE.inc(last)
                        em("tensor", _g)
                        snap(f"opmm{cc}_{gi}", sPE)

                        def _e(vector, cc=cc, tt=tt, mo=mo, gi=gi):
                            wait(vector, f"opmm{cc}_{gi}")
                            if gi == 0 and cc > 0:
                                wait(vector, f"mixdma{cc - 1}")
                            pm = ps[3] if gi % 2 == 0 else psr
                            (vector.tensor_tensor(
                                out=mixp_sb[:, tt, mo * 512:(mo + 1) * 512],
                                in0=pm[:], in1=outbh_sb[:, mo * 512:(mo + 1) * 512],
                                op=ALU.add))
                        em("vector", _e)
                        snap(f"opev{cc}_{gi}", sVE)
                snap(f"mixevac{cc}", sVE)

                def _d(gpsimd, cc=cc):
                    wait(gpsimd, f"mixevac{cc}")
                    dst = mixp_d.ap()[cc * TC:(cc + 1) * TC, :].rearrange(
                        "(tt p) m -> p tt m", p=128)
                    sMixp.inc(gpsimd.dma_start(out=dst, in_=mixp_sb[:]), 16)
                em("gpsimd", _d)
                snap(f"mixdma{cc}", sMixp)

                def _rs(gpsimd, cc=cc):
                    wait(gpsimd, f"mixdma{cc}")
                    sCC.inc(gpsimd.collective_compute(
                        "ReduceScatter", ALU.add,
                        replica_groups=[[0, 1], [2, 3], [4, 5], [6, 7]],
                        ins=[mixp_d.ap()[cc * TC:(cc + 1) * TC, :]],
                        outs=[mixg_d.ap()[cc]]))
                em("gpsimd", _rs)
                snap(f"rs{cc}", sCC)
                snap(f"opdone{cc}", sPE)

            if c > 0:
                emit_outproj(c - 1)

            def _bcst(gpsimd, c=c):
                wait(gpsimd, f"bcevac{c}")
                sBcd.inc(gpsimd.dma_start(out=bc_d.ap()[c % 2],
                                           in_=bc_sb[c % 2][:]), 16)
            em("gpsimd", _bcst)
            snap(f"bcst{c}", sBcd)

            def _bcb(gpsimd, c=c):
                wait(gpsimd, f"bcst{c}")
                if c > 0:
                    wait(gpsimd, f"scanchunk{c - 1}")
                base_el = (c % 2) * 2 * NST * TC
                repb = bass.AP(bc_d.ap().tensor, base_el,
                               [[0, 128], [TC, NST], [1, TC]])
                repc = bass.AP(bc_d.ap().tensor, base_el + NST * TC,
                               [[0, 128], [TC, NST], [1, TC]])
                sBcb.inc(gpsimd.dma_start(out=b_bc[:], in_=repb), 16)
                sBcb.inc(gpsimd.dma_start(out=c_bc[:], in_=repc), 16)
            em("gpsimd", _bcb)
            snap(f"bcast{c}", sBcb)

            for m in range(8):
                for n in range(NST):
                    i = c * 128 + m * 16 + n

                    def _da(scalar, c=c, m=m, n=n, i=i):
                        if i == 0:
                            wait(scalar, "prep")
                        if i >= 2:
                            wait(scalar, f"scan{i - 2}")
                        (scalar.activation(
                            out=da_sb[i % 2][:], in_=dt_sb[:, m, :], func=AF.Exp,
                            scale=a_sb[:, m * 16 + n:m * 16 + n + 1]))
                    em("scalar", _da)
                    snap(f"da{i}", sSC)

                    def _scan(vector, c=c, m=m, n=n, i=i):
                        if n == 0 and m == 0:
                            wait(vector, f"bcast{c}")
                        vector.tensor_tensor(out=dbx_sb[:], in0=u_sb[:, m, :],
                                             in1=b_bc[:, n, :], op=ALU.mult)
                        wait(vector, f"da{i}")
                        init = 0.0 if c == 0 else hstate[:, m * 16 + n:m * 16 + n + 1]
                        vector.tensor_tensor_scan(out=h_all[:, n, :],
                                                  data0=da_sb[i % 2][:],
                                                  data1=dbx_sb[:], initial=init,
                                                  op0=ALU.mult, op1=ALU.add)
                        vector.tensor_copy(hstate[:, m * 16 + n:m * 16 + n + 1],
                                           h_all[:, n, TC - 1:TC])
                        (vector.tensor_tensor(out=h_all[:, n, :],
                                                     in0=h_all[:, n, :],
                                                     in1=c_bc[:, n, :], op=ALU.mult))
                    em("vector", _scan)
                    snap(f"scan{i}", sVE)

                gm = c * 8 + m

                def _red(vector, gm=gm):
                    if gm >= 1:
                        wait(vector, f"ycast1_{gm - 1}")
                    (vector.tensor_reduce(
                        out=y32[:], in_=h_all[:].rearrange("p n t -> p t n"),
                        axis=AX.X, op=ALU.add))
                em("vector", _red)
                snap(f"yred{gm}", sVE)

                def _yc(scalar, c=c, m=m, gm=gm):
                    wait(scalar, f"yred{gm}")
                    if c >= 2 and m == 0:
                        wait(scalar, f"opdone{c - 2}")
                    (scalar.copy(y_bf[c % 2][:, m, :], y32[:]))
                em("scalar", _yc)
                snap(f"ycast1_{gm}", sSC)
            snap(f"scanchunk{c}", sVE)
            snap(f"ycast{c}", sSC)

        emit_outproj(NT - 1)

        snap("allrs", sCC)
        snap("bar_sc", sSC)
        snap("bar_ve", sVE)
        snap("bar_pe", sPE)

        def _bar_plain(eng):
            wait(eng, "allrs")
            eng.sem_inc(sDR.h, 1)

        def _bar_sc(eng):
            wait(eng, "bar_sc")
            wait(eng, "allrs")
            eng.sem_inc(sDR.h, 1)

        def _bar_ve(eng):
            wait(eng, "bar_ve")
            wait(eng, "allrs")
            eng.sem_inc(sDR.h, 1)

        def _bar_pe(eng):
            wait(eng, "bar_pe")
            wait(eng, "allrs")
            eng.sem_inc(sDR.h, 1)
        em("sync", _bar_plain)
        em("gpsimd", _bar_plain)
        em("scalar", _bar_sc)
        em("vector", _bar_ve)
        em("tensor", _bar_pe)
        order.append(("__add__", (sDR, 5)))
        snap("drbar", sDR)

        def _barw(eng):
            eng.wait_ge(sDR.h, 5)
        for name in ("sync", "gpsimd", "scalar", "vector", "tensor"):
            em(name, _barw)

        def _dscld(sync):
            sDsc.inc(sync.dma_start(out=dsc_sb[:], in_=dsc_bc[:]), 16)
        em("sync", _dscld)
        snap("dscld", sDsc)

        for g in range(NG):
            def _gld(sync, g=g):
                if g >= 2:
                    wait(sync, f"mixused{g - 2}")
                src = mixg_d.ap()[g].rearrange("(tt p) m -> p tt m", p=128)
                sMg[g % 2].inc(sync.dma_start(out=mixg_t[g % 2][:], in_=src), 16)
                if g >= 1:
                    wait(sync, f"x1done{g - 1}")
                sX0m.inc(sync.dma_start(out=x0m_t[:],
                                        in_=x0m[g].rearrange("(tt p) m -> p tt m",
                                                             p=128)), 16)
            em("sync", _gld)
            snap(f"gldm{g}", sMg[g % 2])
            snap(f"gldx{g}", sX0m)

            for tt in range(2):
                def _wv(vector, k1=f"gldm{g}", k2=f"gldx{g}"):
                    wait(vector, k1)
                    wait(vector, k2)
                em("vector", _wv)

                def _ws(scalar, k1=f"gldm{g}"):
                    wait(scalar, k1)
                em("scalar", _ws)
                slot, li = emit_ln_stats(lambda g=g, tt=tt: mixg_t[g % 2][:, tt, :],
                                         scr_c)

                def _nmA(scalar, g=g, tt=tt, slot=slot, li=li):
                    wait(scalar, f"lnstat_done{li}")
                    if g >= 1 or tt >= 1:
                        wait(scalar, f"x1_{g}_{tt - 1}" if tt >= 1
                             else f"x1_{g - 1}_1")
                    (scalar.activation(out=scr_c[:],
                                              in_=mixg_t[g % 2][:, tt, 0:1024],
                                              func=AF.Identity,
                                              scale=srstd[:, slot:slot + 1],
                                              bias=snegmr[:, slot:slot + 1]))
                em("scalar", _nmA)
                snap(f"lnmixA{g}_{tt}", sSC)

                def _x1A(vector, g=g, tt=tt):
                    wait(vector, f"lnmixA{g}_{tt}")
                    (vector.tensor_tensor(out=x1_sb[:, tt, 0:1024],
                                                 in0=scr_c[:],
                                                 in1=x0m_t[:, tt, 0:1024], op=ALU.add))
                em("vector", _x1A)
                snap(f"x1a_{g}_{tt}", sVE)

                def _nmB(scalar, g=g, tt=tt, slot=slot):
                    wait(scalar, f"x1a_{g}_{tt}")
                    (scalar.activation(out=scr_c[:],
                                              in_=mixg_t[g % 2][:, tt, 1024:2048],
                                              func=AF.Identity,
                                              scale=srstd[:, slot:slot + 1],
                                              bias=snegmr[:, slot:slot + 1]))
                em("scalar", _nmB)
                snap(f"lnmixB{g}_{tt}", sSC)
                ln_norm_key[li] = f"lnmixB{g}_{tt}"

                def _x1B(vector, g=g, tt=tt):
                    wait(vector, f"lnmixB{g}_{tt}")
                    (vector.tensor_tensor(out=x1_sb[:, tt, 1024:2048],
                                                 in0=scr_c[:],
                                                 in1=x0m_t[:, tt, 1024:2048],
                                                 op=ALU.add))
                em("vector", _x1B)
                snap(f"x1_{g}_{tt}", sVE)

                def _wv2(vector, k=f"x1_{g}_{tt}"):
                    wait(vector, k)
                em("vector", _wv2)

                def _ws2(scalar, k=f"x1_{g}_{tt}"):
                    wait(scalar, k)
                em("scalar", _ws2)
                slot2, li2 = emit_ln_stats(lambda g=g, tt=tt: x1_sb[:, tt, :], scr_c)

                def _nm2(scalar, g=g, tt=tt, slot=slot2, li=li2):
                    wait(scalar, f"lnstat_done{li}")
                    if g >= 1:
                        wait(scalar, f"trB_done_{g - 1}")
                    (scalar.activation(out=ln2_tm[:, tt, :], in_=x1_sb[:, tt, :],
                                              func=AF.Identity,
                                              scale=srstd[:, slot:slot + 1],
                                              bias=snegmr[:, slot:slot + 1]))
                em("scalar", _nm2)
                snap(f"ln2tm{g}_{tt}", sSC)
                ln_norm_key[li2] = f"ln2tm{g}_{tt}"
            snap(f"x1done{g}", sVE)
            snap(f"mixused{g}", sSC)

            for tt in range(2):
                for grp in range(2):
                    gi = g * 4 + tt * 2 + grp

                    def _tr2(tensor, g=g, tt=tt, grp=grp, gi=gi):
                        wait(tensor, f"ln2tm{g}_{tt}")
                        if gi >= 1:
                            wait(tensor, f"trBevac_{gi - 1}")
                        else:
                            wait(tensor, "drbar")
                        if tt == 0 and grp == 0 and g >= 1:
                            wait(tensor, f"ffdone{g - 1}")
                        last = None
                        for dcq in range(8):
                            dc = grp * 8 + dcq
                            last = tensor.transpose(pt0[:, dcq * 128:(dcq + 1) * 128],
                                                    ln2_tm[:, tt, dc * 128:(dc + 1) * 128],
                                                    ident_sb[:])
                        sPE.inc(last)
                    em("tensor", _tr2)
                    snap(f"trB_{gi}", sPE)

                    def _tr2e(scalar, tt=tt, grp=grp, gi=gi):
                        wait(scalar, f"trB_{gi}")
                        dc0 = grp * 8
                        (scalar.copy(
                            ln2_fm[:, dc0:dc0 + 8, tt * 128:(tt + 1) * 128],
                            pt0[:].rearrange("p (dc t) -> p dc t", dc=8)))
                    em("scalar", _tr2e)
                    snap(f"trBevac_{gi}", sSC)
            snap(f"trB_done_{g}", sSC)
            snap(f"ln2fm{g}", sSC)

            def _rsum(tensor, g=g):
                wait(tensor, f"ln2fm{g}")
                if g >= 1:
                    wait(tensor, f"shnev{g - 1}")
                last = None
                for kk in range(16):
                    last = tensor.matmul(psr[0:1, 0:TG], onesc_sb[:, 0:1],
                                         ln2_fm[:, kk, :],
                                         start=(kk == 0), stop=(kk == 15))
                sPE.inc(last)
            em("tensor", _rsum)
            snap(f"rsum{g}", sPE)

            def _rsev(scalar, g=g):
                wait(scalar, f"rsum{g}")
                (scalar.activation(out=rsn_sb[:], in_=psr[0:1, 0:TG],
                                          func=AF.Identity,
                                          scale=ct[0:1, C_NEG128:C_NEG128 + 1]))
            em("scalar", _rsev)
            snap(f"rsnev{g}", sSC)

            for ic in range(64):
                gic = g * 64 + ic

                def _wld(sync, g=g, ic=ic, gic=gic):
                    if gic == 0:
                        wait(sync, "precast")
                    if gic >= 2:
                        wait(sync, f"gumm{gic - 2}")
                    sGU[ic % 2].inc(sync.dma_start(
                        out=wgb[ic % 2][:].rearrange('p k c -> p (k c)'),
                        in_=wg_bf.ap()[ic].rearrange('p k c -> p (k c)')), 16)
                    sGU[ic % 2].inc(sync.dma_start(
                        out=wub[ic % 2][:].rearrange('p k c -> p (k c)'),
                        in_=wu_bf.ap()[ic].rearrange('p k c -> p (k c)')), 16)
                em("sync", _wld)
                snap(f"wld{gic}", sGU[ic % 2])

                def _gu(tensor, g=g, ic=ic, gic=gic):
                    wait(tensor, f"wld{gic}")
                    if ic == 0:
                        wait(tensor, f"rsnev{g}")
                    if gic >= 2:
                        wait(tensor, f"guev{gic - 2}")
                    pg = ps[0] if ic % 2 == 0 else ps[4]
                    pu = ps[1] if ic % 2 == 0 else ps[5]
                    if ic >= 2:
                        wait(tensor, f"hh{g}_{ic - 2}")
                        tensor.matmul(psr[0:1, TG:2 * TG], onesc_sb[:, 0:1],
                                      hh_sb[:, ic - 2, :],
                                      start=(ic - 2 == 0), stop=False)
                    for kk in range(16):
                        tensor.matmul(pg[:, 0:TG], wgb[ic % 2][:, kk, :],
                                      ln2_fm[:, kk, :], start=(kk == 0), stop=False)
                    tensor.matmul(pg[:, 0:TG], onesr_sb[0:1, 0:128], rsn_sb[0:1, :],
                                  start=False, stop=True)
                    for kk in range(16):
                        tensor.matmul(pu[:, 0:TG], wub[ic % 2][:, kk, :],
                                      ln2_fm[:, kk, :], start=(kk == 0), stop=False)
                    sPE.inc(tensor.matmul(pu[:, 0:TG], onesr_sb[0:1, 0:128],
                                          rsn_sb[0:1, :], start=False, stop=True))
                em("tensor", _gu)
                snap(f"gumm{gic}", sPE)

                def _guev(scalar, ic=ic, gic=gic):
                    wait(scalar, f"gumm{gic}")
                    pg = ps[0] if ic % 2 == 0 else ps[4]
                    pu = ps[1] if ic % 2 == 0 else ps[5]
                    scalar.activation(out=g_b[ic % 2][:], in_=pg[:, 0:TG], func=AF.Silu,
                                      scale=gsc_sb[:, ic:ic + 1])
                    (scalar.activation(out=u_b[ic % 2][:], in_=pu[:, 0:TG],
                                              func=AF.Identity,
                                              scale=usc_sb[:, ic:ic + 1]))
                em("scalar", _guev)
                snap(f"guev{gic}", sSC)

                def _hh(vector, g=g, ic=ic, gic=gic):
                    wait(vector, f"guev{gic}")
                    if ic == 0 and g >= 1:
                        wait(vector, f"downdone{g - 1}")
                    (vector.tensor_tensor(out=hh_sb[:, ic, :], in0=g_b[ic % 2][:],
                                                 in1=u_b[ic % 2][:], op=ALU.mult))
                em("vector", _hh)
                snap(f"hh{g}_{ic}", sVE)

            def _shf(tensor, g=g):
                last = None
                for ic in (62, 63):
                    wait(tensor, f"hh{g}_{ic}")
                    last = tensor.matmul(psr[0:1, TG:2 * TG], onesc_sb[:, 0:1],
                                         hh_sb[:, ic, :], start=False, stop=(ic == 63))
                sPE.inc(last)
            em("tensor", _shf)
            snap(f"shmm{g}", sPE)
            snap(f"ffdone{g}", sPE)

            def _shev(scalar, g=g):
                wait(scalar, f"shmm{g}")
                (scalar.activation(out=shn_sb[:], in_=psr[0:1, TG:2 * TG],
                                          func=AF.Identity,
                                          scale=ct[0:1, C_NEG128:C_NEG128 + 1]))
            em("scalar", _shev)
            snap(f"shnev{g}", sSC)

            for ho in range(4):
                gho = g * 4 + ho
                for icg in range(8):
                    gw = gho * 8 + icg

                    def _wdl(sync, ho=ho, icg=icg, gw=gw):
                        if gw >= 2:
                            wait(sync, f"dmm{gw - 2}")
                        if gw == 0:
                            wait(sync, "precast")
                        src = wd_bf.ap()[ho, icg * 8:(icg + 1) * 8].rearrange(
                            "u p c -> p u c")
                        sWd[icg % 2].inc(sync.dma_start(out=wdb[icg % 2][:],
                                                        in_=src), 16)
                    em("sync", _wdl)
                    snap(f"wdl{gw}", sWd[icg % 2])

                    def _dmm(tensor, g=g, ho=ho, icg=icg, gw=gw, gho=gho):
                        wait(tensor, f"wdl{gw}")
                        if icg == 0:
                            wait(tensor, f"shnev{g}")
                            wait(tensor, f"hh{g}_{63}")
                            if gho >= 1:
                                wait(tensor, f"dnev{gho - 1}")
                        last = None
                        for u in range(8):
                            ic = icg * 8 + u
                            for tt in range(2):
                                last = tensor.matmul(
                                    ps[2 + tt][:], hh_sb[:, ic, tt * 128:(tt + 1) * 128],
                                    wdb[icg % 2][:, u, :],
                                    start=(ic == 0), stop=False)
                        if icg == 7:
                            for tt in range(2):
                                last = tensor.matmul(ps[2 + tt][:],
                                                     shn_sb[0:1, tt * 128:(tt + 1) * 128],
                                                     onesr_sb[0:1, 0:512],
                                                     start=False, stop=True)
                        sPE.inc(last)
                    em("tensor", _dmm)
                    snap(f"dmm{gw}", sPE)

                def _dnev(vector, g=g, ho=ho, gho=gho):
                    wait(vector, f"dmm{gho * 8 + 7}")
                    if g == 0 and ho == 0:
                        wait(vector, "dscld")
                    last = None
                    for tt in range(2):
                        last = vector.tensor_tensor(
                            out=ffn_tm[:, tt, ho * 512:(ho + 1) * 512],
                            in0=ps[2 + tt][:], in1=dsc_sb[:, ho * 512:(ho + 1) * 512],
                            op=ALU.mult)
                    (last)
                em("vector", _dnev)
                snap(f"dnev{gho}", sVE)
            snap(f"downdone{g}", sVE)
            snap(f"ffnout{g}", sVE)

            for tt in range(2):
                def _wv3(vector, k=f"ffnout{g}"):
                    wait(vector, k)
                em("vector", _wv3)

                def _ws3(scalar, k=f"ffnout{g}"):
                    wait(scalar, k)
                em("scalar", _ws3)
                slot3, li3 = emit_ln_stats(lambda g=g, tt=tt: ffn_tm[:, tt, :], scr_c)

                def _nm3(scalar, g=g, tt=tt, slot=slot3, li=li3):
                    wait(scalar, f"lnstat_done{li}")
                    if g >= 1 and tt == 0:
                        wait(scalar, f"outdma{g - 1}")
                    (scalar.activation(out=out_st[:, tt, :], in_=ffn_tm[:, tt, :],
                                              func=AF.Identity,
                                              scale=srstd[:, slot:slot + 1],
                                              bias=snegmr[:, slot:slot + 1]))
                em("scalar", _nm3)
                snap(f"ln3_{g}_{tt}", sSC)
                ln_norm_key[li3] = f"ln3_{g}_{tt}"

                def _fin(vector, g=g, tt=tt):
                    wait(vector, f"ln3_{g}_{tt}")
                    (vector.tensor_tensor(out=out_st[:, tt, :],
                                                 in0=out_st[:, tt, :],
                                                 in1=x1_sb[:, tt, :], op=ALU.add))
                em("vector", _fin)
                snap(f"fin{g}_{tt}", sVE)

            def _ost(sync, g=g):
                wait(sync, f"fin{g}_0")
                wait(sync, f"fin{g}_1")
                dst = out[g].rearrange("(tt p) m -> p tt m", p=128)
                sST.inc(sync.dma_start(out=dst, in_=out_st[:]), 16)
            em("sync", _ost)
            snap(f"outdma{g}", sST)

        def _final(sync, n=None):
            sync.wait_ge(sST.h, sST.n)
        em("sync", _final)

        mock = _MockEng()
        mock_sc = ChainEng(mock, sSC)
        mock_ve = ChainEng(mock, sVE)
        for eng_name, payload in order:
            if eng_name == "__snap__":
                key, sem = payload
                ev[key] = (sem, sem.n)
            elif eng_name == "__add__":
                sem, k = payload
                sem.n += k
            elif eng_name == "scalar":
                payload(mock_sc)
            elif eng_name == "vector":
                payload(mock_ve)
            else:
                payload(mock)
        for s_ in ALL_SEMS:
            s_.n = 0

        blk.sync(lambda eng: [f(eng) for f in sched["sync"]])
        blk.gpsimd(lambda eng: [f(eng) for f in sched["gpsimd"]])
        blk.scalar(lambda eng: [f(ChainEng(eng, sSC)) for f in sched["scalar"]])
        blk.vector(lambda eng: [f(ChainEng(eng, sVE)) for f in sched["vector"]])
        blk.tensor(lambda eng: [f(eng) for f in sched["tensor"]])

    return nc


def _prep_inputs(k, x, A_log, x_proj_w, dt_proj_w, dt_proj_b, out_proj_w, shared):
    b, j = k >> 1, k & 1
    d0, d1 = j * DH, (j + 1) * DH
    perm = np.concatenate([np.arange(d0, d1), np.arange((1 - j) * DH, (2 - j) * DH)])

    xbv = np.ascontiguousarray(x[b][:, perm], dtype=np.float32)
    x0mv = np.ascontiguousarray(x[b].reshape(NG, 2, TG, D)[:, j], dtype=np.float32)

    wdtT = dt_proj_w.T
    wdt_k = np.ascontiguousarray(
        wdtT[perm][:, d0:d1].reshape(16, 128, DH).transpose(1, 0, 2), dtype=np.float32)
    wxp_k = np.ascontiguousarray(
        x_proj_w.T[perm].reshape(16, 128, 2 * NST).transpose(1, 0, 2), dtype=np.float32)
    wout_k = np.ascontiguousarray(
        out_proj_w.T[d0:d1].reshape(8, 128, 4, 512).transpose(2, 1, 0, 3),
        dtype=np.float32)
    alog_k = np.ascontiguousarray(
        A_log[d0:d1].reshape(8, 128, NST).transpose(1, 0, 2).reshape(128, 128),
        dtype=np.float32)
    dtb_k = np.ascontiguousarray(dt_proj_b[d0:d1].reshape(8, 128).T, dtype=np.float32)

    return dict(xb=xbv, x0m=x0mv, wdt=wdt_k, wxp=wxp_k, wout=wout_k, alog=alog_k,
                dtb=dtb_k, **shared)


def kernel(L0_attn_x, L0_attn_A_log, L0_attn_x_proj_w, L0_attn_x_proj_b,
           L0_attn_dt_proj_w, L0_attn_dt_proj_b, L0_attn_out_proj_w,
           L0_attn_out_proj_b, L0_ffn_gate_w, L0_ffn_gate_scale, L0_ffn_up_w,
           L0_ffn_up_scale, L0_ffn_down_w, L0_ffn_down_scale):
    x = np.asarray(L0_attn_x, dtype=np.float32)
    A_log = np.asarray(L0_attn_A_log, dtype=np.float32)
    x_proj_w = np.asarray(L0_attn_x_proj_w, dtype=np.float32)
    x_proj_b = np.asarray(L0_attn_x_proj_b, dtype=np.float32)
    dt_proj_w = np.asarray(L0_attn_dt_proj_w, dtype=np.float32)
    dt_proj_b = np.asarray(L0_attn_dt_proj_b, dtype=np.float32)
    out_proj_w = np.asarray(L0_attn_out_proj_w, dtype=np.float32)
    out_proj_b = np.asarray(L0_attn_out_proj_b, dtype=np.float32)
    gate_w = np.asarray(L0_ffn_gate_w)
    up_w = np.asarray(L0_ffn_up_w)
    down_w = np.asarray(L0_ffn_down_w)
    gate_scale = np.asarray(L0_ffn_gate_scale, dtype=np.float32)
    up_scale = np.asarray(L0_ffn_up_scale, dtype=np.float32)
    down_scale = np.asarray(L0_ffn_down_scale, dtype=np.float32)

    wg8v = np.ascontiguousarray(
        gate_w.astype(np.uint8).T.reshape(16, 128, 64, 128).transpose(2, 1, 0, 3))
    wu8v = np.ascontiguousarray(
        up_w.astype(np.uint8).T.reshape(16, 128, 64, 128).transpose(2, 1, 0, 3))
    wd8v = np.ascontiguousarray(
        down_w.astype(np.uint8).T.reshape(64, 128, 4, 512).transpose(2, 0, 1, 3))
    shared = dict(
        xpb=np.ascontiguousarray(x_proj_b.reshape(2 * NST, 1)),
        outb_bc=np.ascontiguousarray(np.broadcast_to(out_proj_b, (128, D))),
        wg8=wg8v, wu8=wu8v, wd8=wd8v,
        gsc=np.ascontiguousarray(gate_scale.reshape(64, 128).T),
        usc=np.ascontiguousarray(up_scale.reshape(64, 128).T),
        dsc_bc=np.ascontiguousarray(np.broadcast_to(down_scale, (128, D))),
        ident=np.eye(128, dtype=BF),
        onesc=np.ones((128, 1), dtype=BF),
        onesr=np.ones((1, 512), dtype=BF),
        consts=np.ascontiguousarray(np.broadcast_to(np.array(
            [1e-5, 1.0 / D, -1.0, 1.0, -0.5, -128.0, 0.5, 0.0], np.float32),
            (128, NCONST))),
    )

    nc = build_nc()
    in_maps = [_prep_inputs(k, x, A_log, x_proj_w, dt_proj_w, dt_proj_b,
                            out_proj_w, shared) for k in range(8)]
    res = run_bass_kernel_spmd(nc, in_maps, core_ids=list(range(8)))

    outv = np.empty((B, L, D), dtype=np.float32)
    for k in range(8):
        b, j = k >> 1, k & 1
        outv[b].reshape(NG, 2, TG, D)[:, j] = res.results[k]["out"]
    return outv


if __name__ == "__main__":
    nc = build_nc()
    print("built; instructions:", nc.n_instructions(),
          "sbuf remaining:", nc.sbuf_bytes_remaining)
